# version 23 (final)
# baseline (speedup 1.0000x reference)
import contextlib

import numpy as np
import ml_dtypes

import concourse.bass as bass
import concourse.mybir as mybir
from concourse.bass_utils import run_bass_kernel_spmd

F32 = mybir.dt.float32
BF16 = mybir.dt.bfloat16
AF = mybir.ActivationFunctionType
ALU = mybir.AluOpType
AX = mybir.AxisListType

D = 2048
DCH = 16
S = 2048
QT = 1024
NH = 16
NKV = 4
NJ = 64
EPS = 1e-5
SCALE = float(1.0 / np.sqrt(np.float32(128.0)))
NEGHALF = -0.5
INV_D = float(np.float32(1.0 / 2048.0))
NEG1 = -1.0

ENGINES = ("tensor", "vector", "scalar", "gpsimd", "sync")


class Sched:

    def __init__(self, nc, es):
        self.nc = nc
        self.es = es
        self.plan = {e: [] for e in ENGINES}
        self.cnt = {}
        self.waited = {e: {} for e in ENGINES}
        self.sems = {}
        self.dma_pools = {}

    def sem(self, name):
        if name not in self.sems:
            self.sems[name] = self.es.enter_context(self.nc.semaphore(name))
            self.cnt[name] = 0
        return self.sems[name]

    def op(self, eng, fn, waits=(), inc=None, n=1):
        for tok in waits:
            if tok is None:
                continue
            s, v = tok
            w = self.waited[eng]
            if w.get(s, 0) < v:
                w[s] = v
                sem = self.sem(s)
                self.plan[eng].append(
                    lambda e, sem=sem, vv=v: e.wait_ge(sem, vv))
        token = None
        if inc is not None:
            sem = self.sem(inc)
            self.cnt[inc] += n
            token = (inc, self.cnt[inc])
            self.plan[eng].append(
                lambda e, fn=fn, sem=sem, nn=n: fn(e).then_inc(sem, nn))
        else:
            self.plan[eng].append(lambda e, fn=fn: fn(e))
        return token

    def wait(self, eng, tok):
        if tok is not None:
            s, v = tok
            w = self.waited[eng]
            if w.get(s, 0) < v:
                w[s] = v
                sem = self.sem(s)
                self.plan[eng].append(lambda e, sem=sem, vv=v: e.wait_ge(sem, vv))

    def dma(self, eng, fn, waits=(), stream="w", nsem=4):
        pool = self.dma_pools.setdefault(stream, {"n": nsem, "i": 0})
        idx = pool["i"] % pool["n"]
        pool["i"] += 1
        name = f"s_{stream}{idx}"
        prev_total = self.cnt.get(name, 0)
        chain = (name, prev_total) if prev_total else None
        return self.op(eng, fn, waits=list(waits) + [chain], inc=name, n=16)

    def flush_block(self):
        plans = self.plan
        self.plan = {e: [] for e in ENGINES}
        active = [(e, plans[e]) for e in ENGINES if plans[e]]
        if not active:
            return
        with self.nc.Block() as blk:
            for e, ops in active:
                def body(eng, ops=ops):
                    for f in ops:
                        f(eng)
                getattr(blk, e)(body)


def later(*toks):
    best = None
    for t in toks:
        if t is None:
            continue
        if best is None or t[1] > best[1]:
            best = t
    return best


def build(debug=False):
    nc = bass.Bass("TRN2", target_bir_lowering=False)

    xO = nc.dram_tensor("xO", [128, DCH * QT], F32, kind="ExternalInput").ap()
    xA = nc.dram_tensor("xA", [128, DCH * 512], F32, kind="ExternalInput").ap()
    xB = nc.dram_tensor("xB", [128, DCH * 512], F32, kind="ExternalInput").ap()
    wq_h = nc.dram_tensor("wq_h", [NH, 128, D], F32, kind="ExternalInput").ap()
    wk_h = nc.dram_tensor("wk_h", [NKV, 128, D], F32, kind="ExternalInput").ap()
    wv = nc.dram_tensor("wv", [D, 512], F32, kind="ExternalInput").ap()
    wo_h = nc.dram_tensor("wo_h", [DCH, 128, D], F32, kind="ExternalInput").ap()
    w0_h = nc.dram_tensor("w0_h", [NJ, 128, D], F32, kind="ExternalInput").ap()
    w1q = nc.dram_tensor("w1q", [32, 128, 4096], F32, kind="ExternalInput").ap()
    bq_c = nc.dram_tensor("bq_c", [128, NH], F32, kind="ExternalInput").ap()
    bk_c = nc.dram_tensor("bk_c", [128, NKV], F32, kind="ExternalInput").ap()
    bv_r = nc.dram_tensor("bv_r", [1, 512], F32, kind="ExternalInput").ap()
    bo_c = nc.dram_tensor("bo_c", [128, DCH], F32, kind="ExternalInput").ap()
    b0_c = nc.dram_tensor("b0_c", [128, NJ], F32, kind="ExternalInput").ap()
    b1_c = nc.dram_tensor("b1_c", [128, DCH], F32, kind="ExternalInput").ap()
    ones_cb = nc.dram_tensor("ones_cb", [128, 1], BF16, kind="ExternalInput").ap()
    ones_sq = nc.dram_tensor("ones_sq", [128, 128], BF16, kind="ExternalInput").ap()
    ones_rb = nc.dram_tensor("ones_rb", [1, 128], BF16, kind="ExternalInput").ap()
    ones_rf = nc.dram_tensor("ones_rf", [1, 128], F32, kind="ExternalInput").ap()
    consts = nc.dram_tensor("consts", [128, 4], F32, kind="ExternalInput").ap()
    outT = nc.dram_tensor("outT", [D, QT], F32, kind="ExternalOutput").ap()
    dbg = {}
    if debug:
        for name, total in [("d_kb", NKV * S), ("d_vb", 16 * 512),
                            ("d_qb", NH * QT), ("d_av", NH * QT),
                            ("d_res", DCH * QT), ("d_xh2", DCH * QT),
                            ("d_dact", 16 * 1024)]:
            dbg[name] = nc.dram_tensor(name, [128, total], F32,
                                       kind="ExternalOutput").ap()

    es = contextlib.ExitStack()
    with es:
        sc = Sched(nc, es)

        RES = es.enter_context(nc.sbuf_tensor("RES", [128, DCH * QT], F32))
        XH = es.enter_context(nc.sbuf_tensor("XH", [128, DCH * QT], BF16))
        bq_sb = es.enter_context(nc.sbuf_tensor("bq_sb", [128, NH], F32))
        bqs_sb = es.enter_context(nc.sbuf_tensor("bqs_sb", [128, NH], F32))
        bk_sb = es.enter_context(nc.sbuf_tensor("bk_sb", [128, NKV], F32))
        bv_sb = es.enter_context(nc.sbuf_tensor("bv_sb", [1, 512], BF16))
        bo_sb = es.enter_context(nc.sbuf_tensor("bo_sb", [128, DCH], F32))
        b0_sb = es.enter_context(nc.sbuf_tensor("b0_sb", [128, NJ], F32))
        b1_sb = es.enter_context(nc.sbuf_tensor("b1_sb", [128, DCH], F32))
        onescb = es.enter_context(nc.sbuf_tensor("onescb", [128, 1], BF16))
        onessq = es.enter_context(nc.sbuf_tensor("onessq", [128, 128], BF16))
        onesrb = es.enter_context(nc.sbuf_tensor("onesrb", [1, 128], BF16))
        onesrf = es.enter_context(nc.sbuf_tensor("onesrf", [1, 128], F32))
        cst_sb = es.enter_context(nc.sbuf_tensor("cst_sb", [128, 4], F32))
        nc.const_aps.register(F32, EPS, cst_sb[:, 0:1])
        nc.const_aps.register(F32, NEGHALF, cst_sb[:, 1:2])
        nc.const_aps.register(F32, SCALE, cst_sb[:, 2:3])
        nc.const_aps.register(F32, NEG1, cst_sb[:, 3:4])

        t_small = []
        for dst, src in [(bq_sb[:, :], bq_c), (bk_sb[:, :], bk_c),
                         (bo_sb[:, :], bo_c), (b0_sb[:, :], b0_c),
                         (b1_sb[:, :], b1_c), (onescb[:, :], ones_cb),
                         (onessq[:, :], ones_sq),
                         (onesrb[:, :], ones_rb), (onesrf[:, :], ones_rf),
                         (cst_sb[:, :], consts)]:
            t_small.append(sc.dma("sync", lambda e, d=dst, s=src: e.dma_start(d, s),
                                  stream="sm", nsem=4))
        tok_small_all = [t_small[-4], t_small[-3], t_small[-2], t_small[-1]]
        t_bv = sc.dma("gpsimd", lambda e: e.dma_start(bv_sb[:, :], bv_r),
                      stream="w")
        t_bqs = sc.op("vector",
                      lambda e: e.tensor_scalar_mul(bqs_sb[:, :], bq_sb[:, :], SCALE),
                      waits=tok_small_all, inc="s_v")

        xb = es.enter_context(nc.sbuf_tensor("xb", [128, 1024], BF16))
        x2b = es.enter_context(nc.sbuf_tensor("x2b", [128, 1024], BF16))
        abc = es.enter_context(nc.sbuf_tensor("abc", [128, 1024], BF16))
        bbc = es.enter_context(nc.sbuf_tensor("bbc", [128, 1024], BF16))
        tmp = (es.enter_context(nc.sbuf_tensor("tmp", [128, 512], F32))
               if debug else None)
        tmp2 = es.enter_context(nc.sbuf_tensor("tmp2", [128, 1024], BF16))
        meanF = es.enter_context(nc.sbuf_tensor("meanF", [128, 512], F32))
        scrF = es.enter_context(nc.sbuf_tensor("scrF", [128, 512], F32))

        s1w = es.enter_context(contextlib.ExitStack())
        wblk = s1w.enter_context(nc.sbuf_tensor("wblk", [128, 4 * D], BF16))
        s1kv = es.enter_context(contextlib.ExitStack())
        KB = s1kv.enter_context(nc.sbuf_tensor("KB", [128, NKV * S], BF16))
        VB = s1kv.enter_context(nc.sbuf_tensor("VB", [128, 16 * 512], BF16))

        wblk_free = [None, None, None, None]

        def load_wblk_at(slot, src2d):
            dst = wblk[:, slot * D:(slot + 1) * D]
            return sc.dma("gpsimd", lambda e, d=dst, s=src2d: e.dma_start(d, s),
                          waits=[wblk_free[slot]], stream="w")

        def emit_ln(src_of, dst_off, width, tag, src_ready_of=None,
                    mm_extra=None):
            nsub = max(1, width // 512)
            subw = min(512, width)
            ps = contextlib.ExitStack()
            t_rstd = {}
            t_cpa = {}
            t_cpb = {}
            with ps:
                SA = ps.enter_context(nc.psum_tensor("psA" + tag, [128, width], F32))
                SB = ps.enter_context(nc.psum_tensor("psB" + tag, [128, width], F32))
                mm_toks = []
                sub_done = {}
                for dc in range(DCH):
                    rdy = list(src_ready_of(dc)) if src_ready_of else []
                    for sub in range(nsub):
                        i = dc * nsub + sub
                        seg = i % 2
                        xb_seg = xb[:, seg * 512:seg * 512 + subw]
                        x2b_seg = x2b[:, seg * 512:seg * 512 + subw]
                        src_seg = src_of(dc)[:, sub * subw:(sub + 1) * subw]
                        guard = mm_toks[-2] if len(mm_toks) >= 2 else None
                        t_cp = sc.op("vector",
                                     lambda e, d=xb_seg, s=src_seg: e.tensor_copy(d, s),
                                     waits=rdy + [guard], inc="s_v")
                        t_sq = sc.op("scalar",
                                     lambda e, d=x2b_seg, s=src_seg:
                                     e.activation(d, s, AF.Square),
                                     waits=rdy + [guard], inc="s_a")
                        st = (dc == 0)
                        sp = (dc == DCH - 1)
                        sc.op("tensor",
                              lambda e, d=SA[:, sub * subw:(sub + 1) * subw],
                              r=xb_seg, st=st, sp=sp:
                              e.matmul(d, onessq[:, :], r, start=st, stop=sp),
                              waits=[t_cp, mm_extra] + tok_small_all)
                        mm_toks.append(sc.op(
                            "tensor",
                            lambda e, d=SB[:, sub * subw:(sub + 1) * subw],
                            r=x2b_seg, st=st, sp=sp:
                            e.matmul(d, onessq[:, :], r, start=st, stop=sp),
                            waits=[t_sq], inc="s_t"))
                        if sp:
                            sub_done[sub] = mm_toks[-1]
                for sub in range(nsub):
                    lo = sub * subw
                    sl = slice(lo, lo + subw)
                    s0 = slice(0, subw)
                    ssl = slice(sub * subw, (sub + 1) * subw)
                    t_mean = sc.op("vector",
                                   lambda e, s0=s0, ssl=ssl: e.tensor_scalar_mul(
                                       meanF[:, s0], SA[:, ssl], INV_D),
                                   waits=[sub_done[sub]], inc="s_v")
                    t_ex2 = sc.op("vector",
                                  lambda e, s0=s0, ssl=ssl: e.tensor_scalar_mul(
                                      scrF[:, s0], SB[:, ssl], INV_D),
                                  waits=[t_mean], inc="s_v")
                    t_nm2 = sc.op("vector",
                                  lambda e, s0=s0, ssl=ssl: e.scalar_tensor_tensor(
                                      SA[:, ssl], meanF[:, s0], -1.0,
                                      meanF[:, s0], ALU.mult, ALU.mult),
                                  waits=[t_ex2], inc="s_v")
                    t_var = sc.op("vector",
                                  lambda e, s0=s0, ssl=ssl: e.tensor_tensor(
                                      SB[:, ssl], scrF[:, s0],
                                      SA[:, ssl], ALU.add),
                                  waits=[t_nm2], inc="s_v")
                    t_lnv = sc.op("scalar",
                                  lambda e, ssl=ssl: e.activation(
                                      SA[:, ssl], SB[:, ssl], AF.Ln, bias=EPS),
                                  waits=[t_var], inc="s_a")
                    t_rstd[sub] = sc.op("scalar",
                                        lambda e, s0=s0, ssl=ssl: e.activation(
                                            scrF[:, s0], SA[:, ssl], AF.Exp,
                                            scale=NEGHALF),
                                        waits=[t_lnv, t_cpb.get(sub - 1)],
                                        inc="s_a")
                    t_cpa[sub] = sc.op("vector",
                                       lambda e, sl=sl, s0=s0: e.tensor_copy(
                                           abc[:, sl], scrF[:, s0]),
                                       waits=[t_rstd[sub]], inc="s_v")
                    t_cpb[sub] = sc.op("vector",
                                       lambda e, sl=sl, s0=s0:
                                       e.scalar_tensor_tensor(
                                           bbc[:, sl], meanF[:, s0], -1.0,
                                           scrF[:, s0], ALU.mult, ALU.mult),
                                       waits=[t_cpa[sub]], inc="s_v")
            adds = []
            adds_map = {}
            for sub in range(nsub):
                lo = sub * subw
                for dc in range(DCH):
                    seg = dc % 2
                    tseg = tmp2[:, seg * 512:seg * 512 + subw]
                    src_seg = src_of(dc)[:, lo:lo + subw]
                    dst = XH[:, dst_off(dc) + lo: dst_off(dc) + lo + subw]
                    guard = adds[-2] if len(adds) >= 2 else None
                    last_mul = sc.op("vector",
                                     lambda e, t=tseg, s=src_seg, lo=lo:
                                     e.tensor_tensor(t, s, abc[:, lo:lo + subw],
                                                     ALU.mult),
                                     waits=[t_cpa[sub], t_cpb[sub], guard],
                                     inc="s_v")
                    last_add = sc.op("vector",
                                     lambda e, d=dst, t=tseg, lo=lo:
                                     e.tensor_tensor(d, t, bbc[:, lo:lo + subw],
                                                     ALU.add),
                                     waits=[last_mul], inc="s_v")
                    adds.append(last_add)
                    adds_map[(sub, dc)] = last_add
            return adds[-1], adds_map, t_rstd[0], t_rstd[nsub - 1]

        def emit_kv(half, wvst, adds, bank_toks):
            k_ev = {}
            psK = contextlib.ExitStack()
            with psK:
                ps_k = [psK.enter_context(
                    nc.psum_tensor(f"ps_k{half}{p}", [128, 1024], F32))
                    for p in range(2)]
                loads = {}
                for kc in range(NKV):
                    loads[kc] = (kc, load_wblk_at(kc, wk_h[kc]))
                for sub in range(2):
                    for kc in range(NKV):
                        slot, t_load = loads[kc]
                        slot2 = kc % 2
                        t_mm = None
                        for dc in range(DCH):
                            t_mm = sc.op(
                                "tensor",
                                lambda e, slot=slot, slot2=slot2, dc=dc, sub=sub:
                                e.matmul(
                                    ps_k[slot2][:, sub * 512:(sub + 1) * 512],
                                    wblk[:, slot * D + dc * 128:slot * D + (dc + 1) * 128],
                                    XH[:, dc * 1024 + sub * 512:dc * 1024 + (sub + 1) * 512],
                                    start=(dc == 0), stop=(dc == DCH - 1)),
                                waits=[t_load, adds[(sub, dc)],
                                       k_ev.get((slot2, sub))]
                                      + (bank_toks if (kc == 0 and dc == 0) else []),
                                inc="s_t" if dc == DCH - 1 else None)
                        k_ev[(slot2, sub)] = sc.op(
                            "scalar",
                            lambda e, kc=kc, slot2=slot2, sub=sub, half=half:
                            e.activation(
                                KB[:, kc * S + half * 1024 + sub * 512:
                                   kc * S + half * 1024 + (sub + 1) * 512],
                                ps_k[slot2][:, sub * 512:(sub + 1) * 512],
                                AF.Identity, bias=bk_sb[:, kc:kc + 1]),
                            waits=[t_mm], inc="s_a")
                        if sub == 1:
                            wblk_free[slot] = t_mm
            k_done = later(*k_ev.values())
            psV = contextlib.ExitStack()
            with psV:
                ps_v = [psV.enter_context(
                    nc.psum_tensor(f"ps_v{half}{k}", [128, 512], F32))
                    for k in range(8)]
                v_ev = [None] * 8
                wv_free = [None] * 4
                wv_loaded = {}

                def load_wv(dc):
                    slot = dc % 4
                    wv_loaded[dc] = sc.dma(
                        "gpsimd",
                        lambda e, dc=dc, slot=slot: e.dma_start(
                            wvst[:, slot * 512:(slot + 1) * 512],
                            wv[dc * 128:(dc + 1) * 128, :]),
                        waits=[wv_free[slot]], stream="w")

                for dc in range(4):
                    load_wv(dc)
                t_last = None
                for dc in range(DCH):
                    slot = dc % 4
                    for j in range(8):
                        t_last = sc.op(
                            "tensor",
                            lambda e, dc=dc, kt=j, slot=slot, j=j:
                            e.matmul(
                                ps_v[j][:, :],
                                XH[:, dc * 1024 + kt * 128:dc * 1024 + kt * 128 + 128],
                                wvst[:, slot * 512:(slot + 1) * 512],
                                start=(dc == 0), stop=False),
                            waits=[wv_loaded[dc], adds[(j // 4, dc)],
                                   k_done if dc == 0 else None]
                                  + (bank_toks if dc == 0 else []),
                            inc="s_t" if j == 7 else None)
                    wv_free[slot] = t_last
                    if dc + 4 < DCH:
                        load_wv(dc + 4)
                for j in range(8):
                    gkt = half * 8 + j
                    t_r1 = sc.op("tensor",
                                 lambda e, j=j: e.matmul(
                                     ps_v[j][:, :], onesrb[:, :], bv_sb[:, :],
                                     start=False, stop=True),
                                 waits=[t_last, t_bv], inc="s_t")
                    v_ev[j] = sc.op(
                        "scalar",
                        lambda e, gkt=gkt, j=j: e.activation(
                            VB[:, gkt * 512:(gkt + 1) * 512],
                            ps_v[j][:, :], AF.Copy),
                        waits=[t_r1], inc="s_a")
                toks = v_ev + list(k_ev.values())
            return toks

        sA = es.enter_context(contextlib.ExitStack())
        XS = sA.enter_context(nc.sbuf_tensor("XS", [128, DCH * 512], F32))
        wvstA = sA.enter_context(nc.sbuf_tensor("wvstA", [128, 4 * 512], BF16))

        adds_other = {}
        bank_toks_A = []
        t_res_load = None
        prev_amap = None
        for stripe in range(2):
            xsrc = xA if stripe == 0 else xB
            t_xs = {}
            for dc in range(DCH):
                w = [prev_amap[(0, dc)]] if prev_amap else []
                t_xs[dc] = sc.dma(
                    "sync",
                    lambda e, dc=dc, xsrc=xsrc: e.dma_start(
                        XS[:, dc * 512:(dc + 1) * 512],
                        xsrc[:, dc * 512:(dc + 1) * 512]),
                    waits=w, stream="x", nsem=4)
            if stripe == 0:
                t_res_load = sc.dma(
                    "sync",
                    lambda e: e.dma_start(RES[:, :], xO),
                    stream="x", nsem=4)
            _, amap, t_ca, t_cb = emit_ln(
                lambda dc: XS[:, dc * 512:(dc + 1) * 512],
                lambda dc, s=stripe: dc * 1024 + s * 512,
                512, f"A{stripe}", src_ready_of=lambda dc: [t_xs[dc]],
                mm_extra=(bank_toks_A[1] if stripe else None))
            for dc in range(DCH):
                adds_other[(stripe, dc)] = amap[(0, dc)]
            bank_toks_A = [t_ca, t_cb]
            prev_amap = amap
        kv_other = emit_kv(1, wvstA, adds_other, bank_toks_A)

        sA.close()
        sB = es.enter_context(contextlib.ExitStack())
        wvstB = sB.enter_context(nc.sbuf_tensor("wvstB", [128, 4 * 512], BF16))
        QB = sB.enter_context(nc.sbuf_tensor("QB", [128, NH * QT], BF16))

        kv_other_done = later(*kv_other)
        xh_own, adds_own, t_caB, t_cbB = emit_ln(
            lambda dc: RES[:, dc * 1024:(dc + 1) * 1024],
            lambda dc: dc * 1024, 1024, "B",
            src_ready_of=lambda dc: [t_res_load] + bank_toks_A,
            mm_extra=kv_other_done)
        kv_own = emit_kv(0, wvstB, adds_own, [t_caB, t_cbB])

        psQ = contextlib.ExitStack()
        with psQ:
            ps_q = [psQ.enter_context(nc.psum_tensor(f"ps_q{p}", [128, 1024], F32))
                    for p in range(2)]
            q_ev = {}
            kv_own_done = later(*kv_own)
            qi = [0]
            qloads = {}
            qslots = {}

            def qload(hc):
                s3 = qi[0] % 4
                qloads[hc] = load_wblk_at(s3, wq_h[hc])
                qslots[hc] = s3
                qi[0] += 1

            qload(0)
            qload(1)
            qload(2)
            for hc in range(NH):
                slot4 = qslots[hc]
                par = hc % 2
                t_mm = None
                for sub in range(2):
                    for dc in range(DCH):
                        t_mm = sc.op(
                            "tensor",
                            lambda e, slot4=slot4, par=par, dc=dc, sub=sub:
                            e.matmul(
                                ps_q[par][:, sub * 512:(sub + 1) * 512],
                                wblk[:, slot4 * D + dc * 128:slot4 * D + (dc + 1) * 128],
                                XH[:, dc * 1024 + sub * 512:dc * 1024 + (sub + 1) * 512],
                                start=(dc == 0), stop=(dc == DCH - 1)),
                            waits=[qloads[hc], adds_own[(sub, dc)],
                                   q_ev.get((par, sub)), kv_own_done],
                            inc="s_t" if dc == DCH - 1 else None)
                    q_ev[(par, sub)] = sc.op(
                        "scalar",
                        lambda e, hc=hc, par=par, sub=sub: e.activation(
                            QB[:, hc * QT + sub * 512:hc * QT + sub * 512 + 512],
                            ps_q[par][:, sub * 512:(sub + 1) * 512],
                            AF.Identity, bias=bqs_sb[:, hc:hc + 1], scale=SCALE),
                        waits=[t_mm, t_bqs], inc="s_a")
                wblk_free[slot4] = t_mm
                if hc + 3 < NH:
                    qload(hc + 3)
            q_ev = list(q_ev.values())

        oloads = {d: load_wblk_at(d % 4, wo_h[d]) for d in range(4)}

        sC = es.enter_context(contextlib.ExitStack())
        NPT = 4 if debug else 6
        PT = sC.enter_context(nc.sbuf_tensor("PT", [128, NPT * 512], BF16))
        rc_sb = sC.enter_context(nc.sbuf_tensor("rc_sb", [128, 1024], F32))
        ps_st = sC.enter_context(nc.psum_tensor("ps_st", [128, 4 * 512], F32))
        ps_u = sC.enter_context(nc.psum_tensor("ps_u", [128, 2 * 512], F32))
        ps_bc = sC.enter_context(nc.psum_tensor("ps_bc", [128, 2 * 512], F32))

        NIT = 32

        T = {k: [None] * NIT for k in ("rc", "avmul")}
        Texp = [[None] * 16 for _ in range(NIT)]
        Tst = [[None] * 16 for _ in range(NIT)]
        Tus = [[None] * 16 for _ in range(NIT)]

        def q_off(h):
            return (h >> 1) * QT + (h & 1) * 512

        def kv_of(h):
            return (h >> 1) // 4

        def emit_st(h, c):
            qo, kv = q_off(h), kv_of(h)
            bank_guard = (Texp[h][c - 4] if c >= 4 else
                          (Texp[h - 1][12 + c] if h >= 1 else qev_last))
            Tst[h][c] = sc.op(
                "tensor",
                lambda e, c=c, qo=qo, kv=kv:
                e.matmul(ps_st[:, (c % 4) * 512:(c % 4) * 512 + 512],
                         KB[:, kv * S + c * 128:kv * S + (c + 1) * 128],
                         QB[:, qo:qo + 512], start=True, stop=True),
                waits=[bank_guard], inc="s_t")

        def emit_exp(h, c):
            ring_guard = (Tus[h][c - NPT] if c >= NPT else
                          (Tus[h - 1][16 - NPT + c] if h >= 1 else None))
            Texp[h][c] = sc.op(
                "scalar",
                lambda e, c=c: e.activation(
                    PT[:, (c % NPT) * 512:(c % NPT) * 512 + 512],
                    ps_st[:, (c % 4) * 512:(c % 4) * 512 + 512],
                    AF.Exp),
                waits=[Tst[h][c], ring_guard],
                inc="s_a")

        def emit_usums(h, c):
            kv = kv_of(h)
            par = h % 2
            pt_sl = PT[:, (c % NPT) * 512:(c % NPT) * 512 + 512]
            sc.op(
                "tensor",
                lambda e, c=c, kv=kv, par=par, pt=pt_sl:
                e.matmul(ps_u[:, par * 512:par * 512 + 512],
                         VB[:, c * 512 + kv * 128:c * 512 + kv * 128 + 128],
                         pt, start=(c == 0), stop=(c == 15)),
                waits=[Texp[h][c],
                       (T["avmul"][h - 2] if h >= 2 else qev_last)
                       if c == 0 else None])
            Tus[h][c] = sc.op(
                "tensor",
                lambda e, c=c, par=par, pt=pt_sl:
                e.matmul(ps_bc[:, par * 512:par * 512 + 512],
                         onessq[:, :], pt,
                         start=(c == 0), stop=(c == 15)),
                waits=[(T["rc"][h - 2] if h >= 2 else qev_last)
                       if c == 0 else None],
                inc="s_t")

        def emit_tail(h):
            head, qt = h >> 1, h & 1
            par = h % 2
            av = XH[:, head * QT + qt * 512: head * QT + qt * 512 + 512]
            T["rc"][h] = sc.op(
                "vector",
                lambda e, par=par: e.reciprocal(
                    rc_sb[:, par * 512:par * 512 + 512],
                    ps_bc[:, par * 512:par * 512 + 512]),
                waits=[Tus[h][15]], inc="s_v")
            T["avmul"][h] = sc.op(
                "vector",
                lambda e, av=av, par=par: e.tensor_tensor(
                    av, ps_u[:, par * 512:par * 512 + 512],
                    rc_sb[:, par * 512:par * 512 + 512], ALU.mult),
                waits=[T["rc"][h]], inc="s_v")

        def dump(name, src_sb, total, ready_toks):
            st = tmp
            prev = None
            for off in range(0, total, 512):
                t_c = sc.op("vector",
                            lambda e, o=off, st=st:
                            e.tensor_copy(st[:, :], src_sb[:, o:o + 512]),
                            waits=list(ready_toks) + [prev], inc="s_v")
                prev = sc.dma("sync",
                              lambda e, o=off, st=st:
                              e.dma_start(dbg[name][:, o:o + 512], st[:, :]),
                              waits=[t_c], stream="dbg", nsem=2)
            sc.wait("sync", prev)
            sc.flush_block()

        if debug:
            sc.flush_block()
            dump("d_kb", KB, NKV * S, kv_other + kv_own)
            dump("d_vb", VB, 16 * 512, kv_other + kv_own)
            dump("d_qb", QB, NH * QT, q_ev)

        qev_last = later(*(q_ev + kv_own))
        for h in range(NIT):
            for c in range(16):
                emit_st(h, c)
                if c >= 2:
                    emit_usums(h, c - 2)
                emit_exp(h, c)
            emit_usums(h, 14)
            emit_usums(h, 15)
            emit_tail(h)

        sc.flush_block()
        if debug:
            dump("d_av", XH, NH * QT, [T["avmul"][NIT - 1]])
        sC.close()

        sB.close()
        s1kv.close()
        sF = es.enter_context(contextlib.ExitStack())
        Dact = sF.enter_context(nc.sbuf_tensor("Dact", [128, 16 * 1024], BF16))
        w0st = sF.enter_context(nc.sbuf_tensor("w0st", [128, 3 * D], BF16))
        w1st = sF.enter_context(nc.sbuf_tensor("w1st", [128, 2 * 4096], BF16))

        arena_tok = later(Tst[NIT - 1][15], Tus[NIT - 1][15])
        arena_tok_v = T["avmul"][NIT - 1]

        w0_free = [None] * 3
        w1_free = [None] * 2
        w0_loaded = {}
        w1_loaded = {}

        def load_w0(nj):
            slot = nj % 3
            w0_loaded[nj] = sc.dma(
                "gpsimd",
                lambda e, nj=nj, slot=slot: e.dma_start(
                    w0st[:, slot * D:(slot + 1) * D], w0_h[nj]),
                waits=[w0_free[slot], arena_tok, arena_tok_v], stream="w")

        def load_w1(q, og):
            slot = og % 2
            w1_loaded[(q, og)] = sc.dma(
                "gpsimd",
                lambda e, q=q, og=og, slot=slot: e.dma_start(
                    w1st[:, slot * 4096:(slot + 1) * 4096], w1q[q * 8 + og]),
                waits=[w1_free[slot], arena_tok, arena_tok_v], stream="w1",
                nsem=4)

        av_all = T["avmul"]
        res1_ev = [None] * DCH
        o_last_mm = None
        psO = contextlib.ExitStack()
        with psO:
            ps_o = [psO.enter_context(nc.psum_tensor(f"ps_o{p}", [128, 1024], F32))
                    for p in range(2)]
            o_ev = [None, None]
            for dc in range(DCH):
                slot4 = dc % 4
                slot = dc % 2
                t_mm = None
                for hc in range(NH):
                    for sub in range(2):
                        t_mm = sc.op(
                            "tensor",
                            lambda e, slot4=slot4, slot=slot, hc=hc, sub=sub:
                            e.matmul(
                                ps_o[slot][:, sub * 512:(sub + 1) * 512],
                                wblk[:, slot4 * D + hc * 128:slot4 * D + (hc + 1) * 128],
                                XH[:, hc * QT + sub * 512:hc * QT + (sub + 1) * 512],
                                start=(hc == 0), stop=(hc == NH - 1)),
                            waits=[oloads[dc], av_all[2 * hc + sub], o_ev[slot],
                                   T["avmul"][NIT - 1] if (dc == 0 and hc == 0)
                                   else None],
                            inc="s_t" if (hc == NH - 1 and sub == 1) else None)
                wblk_free[slot4] = t_mm
                o_last_mm = t_mm
                if dc + 4 < DCH:
                    oloads[dc + 4] = load_wblk_at((dc + 4) % 4, wo_h[dc + 4])
                if dc == 0:
                    load_w0(0)
                elif dc == 1:
                    load_w1(0, 0)
                elif dc == 3:
                    load_w0(1)
                elif dc == 5:
                    load_w1(0, 1)
                elif dc == 7:
                    load_w0(2)
                o_ev[slot] = sc.op(
                    "vector",
                    lambda e, dc=dc, slot=slot: e.scalar_tensor_tensor(
                        RES[:, dc * QT:(dc + 1) * QT], ps_o[slot][:, :],
                        bo_sb[:, dc:dc + 1], RES[:, dc * QT:(dc + 1) * QT],
                        ALU.add, ALU.add),
                    waits=[t_mm], inc="s_v")
                res1_ev[dc] = o_ev[slot]

            if debug:
                sc.flush_block()
                dump("d_res", RES, DCH * QT, [later(*res1_ev)])

            xh2, adds2, t_caE, t_cbE = emit_ln(
                lambda dc: RES[:, dc * 1024:(dc + 1) * 1024],
                lambda dc: dc * 1024, 1024, "C",
                src_ready_of=lambda dc: [res1_ev[dc]])

        if debug:
            sc.flush_block()
            dump("d_xh2", XH, DCH * QT, [xh2])

        ps_f1 = [sF.enter_context(nc.psum_tensor(f"ps_f1{p}", [128, 512], F32))
                 for p in range(4)]
        ps_f2 = [sF.enter_context(nc.psum_tensor(f"ps_f2{p}", [128, 512], F32))
                 for p in range(4)]

        f1_ev = {}
        f1_mm = {}
        f2_bank_free = {}
        f2_last_prev = None
        out_last = None

        for q in range(4):
            for j16 in range(16):
                nj = q * 16 + j16
                slot = nj % 3
                par = j16 % 2
                t_mm_sub = [None, None]
                for dc in range(DCH):
                    for sub in range(2):
                        t_mm_sub[sub] = sc.op(
                            "tensor",
                            lambda e, slot=slot, dc=dc, par=par, sub=sub:
                            e.matmul(
                                ps_f1[par * 2 + sub][:, :],
                                w0st[:, slot * D + dc * 128:slot * D + (dc + 1) * 128],
                                XH[:, dc * 1024 + sub * 512:dc * 1024 + sub * 512 + 512],
                                start=(dc == 0), stop=(dc == DCH - 1)),
                            waits=[w0_loaded[nj], adds2[(sub, dc)],
                                   f1_ev.get((nj - 2, sub)),
                                   t_caE if (q == 0 and j16 < 2 and dc == 0)
                                   else None,
                                   t_cbE if (q == 0 and j16 < 2 and dc == 0)
                                   else None],
                            inc="s_t" if dc == DCH - 1 else None)
                w0_free[slot] = t_mm_sub[1]
                if nj + 3 < NJ:
                    load_w0(nj + 3)
                for sub in range(2):
                    f1_ev[(nj, sub)] = sc.op(
                        "scalar",
                        lambda e, nj=nj, j16=j16, par=par, sub=sub: e.activation(
                            Dact[:, j16 * 1024 + sub * 512:j16 * 1024 + sub * 512 + 512],
                            ps_f1[par * 2 + sub][:, :],
                            AF.Gelu, bias=b0_sb[:, nj:nj + 1]),
                        waits=[t_mm_sub[sub], f2_last_prev], inc="s_a")

            if debug and q == 0:
                sc.flush_block()
                dump("d_dact", Dact, 16 * 1024, [f1_ev[(15, 1)]])

            for og in range(8):
                slot = og % 2
                t_grp = None
                for o2 in range(2):
                    for sub in range(2):
                        bank = o2 * 2 + sub
                        for j16 in range(16):
                            nj = q * 16 + j16
                            t_grp = sc.op(
                                "tensor",
                                lambda e, slot=slot, j16=j16, o2=o2, sub=sub,
                                bank=bank:
                                e.matmul(
                                    ps_f2[bank][:, :],
                                    w1st[:, slot * 4096 + j16 * 256 + o2 * 128:
                                         slot * 4096 + j16 * 256 + o2 * 128 + 128],
                                    Dact[:, j16 * 1024 + sub * 512:
                                         j16 * 1024 + sub * 512 + 512],
                                    start=(j16 == 0), stop=(j16 == 15)),
                                waits=[w1_loaded[(q, og)], f1_ev[(nj, sub)],
                                       f2_bank_free.get(bank) if j16 == 0
                                       else None],
                                inc="s_t" if j16 == 15 else None)
                        hc = og * 2 + o2
                        rs = RES[:, hc * QT + sub * 512:hc * QT + sub * 512 + 512]
                        if q == 0:
                            t_ev = sc.op(
                                "vector",
                                lambda e, rs=rs, bank=bank, hc=hc:
                                e.scalar_tensor_tensor(
                                    rs, ps_f2[bank][:, :], b1_sb[:, hc:hc + 1],
                                    rs, ALU.add, ALU.add),
                                waits=[t_grp], inc="s_v")
                        else:
                            t_ev = sc.op(
                                "vector",
                                lambda e, rs=rs, bank=bank:
                                e.tensor_tensor(rs, ps_f2[bank][:, :], rs,
                                                ALU.add),
                                waits=[t_grp], inc="s_v")
                        f2_bank_free[bank] = t_ev
                        if q == 3 and sub == 1:
                            out_last = sc.dma(
                                "sync",
                                lambda e, hc=hc: e.dma_start(
                                    outT[hc * 128:(hc + 1) * 128, :],
                                    RES[:, hc * QT:(hc + 1) * QT]),
                                waits=[t_ev], stream="o", nsem=2)
                w1_free[slot] = t_grp
                if og + 2 < 8:
                    load_w1(q, og + 2)
                elif q < 3:
                    load_w1(q + 1, og - 6)
            f2_last_prev = t_grp

        for idx in range(sc.dma_pools.get("o", {"n": 2})["n"]):
            name = f"s_o{idx}"
            if name in sc.cnt:
                sc.wait("sync", (name, sc.cnt[name]))
        sc.flush_block()

    return nc


def _host_prep(inp):
    f32 = np.float32
    x = np.asarray(inp["L0_attn_x"], f32)
    Wq = np.asarray(inp["L0_attn_Wq"], f32)
    bq = np.asarray(inp["L0_attn_bq"], f32)
    Wk = np.asarray(inp["L0_attn_Wk"], f32)
    bk = np.asarray(inp["L0_attn_bk"], f32)
    Wv = np.asarray(inp["L0_attn_Wv"], f32)
    bv = np.asarray(inp["L0_attn_bv"], f32)
    Wo = np.asarray(inp["L0_attn_Wo"], f32)
    bo = np.asarray(inp["L0_attn_bo"], f32)
    W0 = np.asarray(inp["L0_ffn_W0"], f32)
    b0 = np.asarray(inp["L0_ffn_bias0"], f32)
    W1 = np.asarray(inp["L0_ffn_W1"], f32)
    b1 = np.asarray(inp["L0_ffn_bias1"], f32)

    shared = {
        "wq_h": np.ascontiguousarray(
            Wq.reshape(DCH, 128, NH, 128).transpose(2, 1, 0, 3).reshape(NH, 128, D)),
        "wk_h": np.ascontiguousarray(
            Wk.reshape(DCH, 128, NKV, 128).transpose(2, 1, 0, 3).reshape(NKV, 128, D)),
        "wv": np.ascontiguousarray(Wv),
        "wo_h": np.ascontiguousarray(
            Wo.reshape(NH, 128, DCH, 128).transpose(2, 1, 0, 3).reshape(DCH, 128, D)),
        "w0_h": np.ascontiguousarray(
            W0.reshape(NJ, 128, DCH, 128).transpose(0, 3, 2, 1).reshape(NJ, 128, D)),
        "w1q": np.ascontiguousarray(
            W1.reshape(8, 2, 128, 4, 16, 128).transpose(3, 0, 5, 4, 1, 2)
            .reshape(32, 128, 4096)),
        "bq_c": np.ascontiguousarray(bq.reshape(NH, 128).T),
        "bk_c": np.ascontiguousarray(bk.reshape(NKV, 128).T),
        "bv_r": np.ascontiguousarray(bv.reshape(1, 512)),
        "bo_c": np.ascontiguousarray(bo.reshape(DCH, 128).T),
        "b0_c": np.ascontiguousarray(b0.reshape(NJ, 128).T),
        "b1_c": np.ascontiguousarray(b1.reshape(DCH, 128).T),
        "ones_cb": np.ones((128, 1), ml_dtypes.bfloat16),
        "ones_sq": np.ones((128, 128), ml_dtypes.bfloat16),
        "ones_rb": np.ones((1, 128), ml_dtypes.bfloat16),
        "ones_rf": np.ones((1, 128), f32),
        "consts": np.broadcast_to(
            np.array([EPS, NEGHALF, SCALE, NEG1], f32), (128, 4)).copy(),
    }
    in_maps = []
    for c in range(8):
        b, h = c // 2, c % 2
        A = np.ascontiguousarray(x[b].reshape(S, DCH, 128).transpose(2, 1, 0))
        o0 = (1 - h) * QT
        m = dict(shared)
        m["xO"] = np.ascontiguousarray(
            A[:, :, h * QT:(h + 1) * QT]).reshape(128, DCH * QT)
        m["xA"] = np.ascontiguousarray(
            A[:, :, o0:o0 + 512]).reshape(128, DCH * 512)
        m["xB"] = np.ascontiguousarray(
            A[:, :, o0 + 512:o0 + 1024]).reshape(128, DCH * 512)
        in_maps.append(m)
    return in_maps


def kernel(**inp):
    in_maps = _host_prep(inp)
    nc = build()
    res = run_bass_kernel_spmd(nc, in_maps, core_ids=list(range(8)))
    out = np.empty((4, S, D), np.float32)
    for c in range(8):
        b, h = c // 2, c % 2
        out[b, h * QT:(h + 1) * QT, :] = res.results[c]["outT"].T
    return out


# version 24
# speedup vs baseline: 1.1875x; 1.1875x over previous
import contextlib

import numpy as np
import ml_dtypes

import concourse.bass as bass
import concourse.mybir as mybir
from concourse.bass_utils import run_bass_kernel_spmd

F32 = mybir.dt.float32
BF16 = mybir.dt.bfloat16
AF = mybir.ActivationFunctionType
ALU = mybir.AluOpType
AX = mybir.AxisListType

D = 2048
DCH = 16
S = 2048
QT = 1024
NH = 16
NKV = 4
NJ = 64
EPS = 1e-5
SCALE = float(1.0 / np.sqrt(np.float32(128.0)))
NEGHALF = -0.5
INV_D = float(np.float32(1.0 / 2048.0))
NEG1 = -1.0

ENGINES = ("tensor", "vector", "scalar", "gpsimd", "sync")


class Sched:

    def __init__(self, nc, es):
        self.nc = nc
        self.es = es
        self.plan = {e: [] for e in ENGINES}
        self.cnt = {}
        self.waited = {e: {} for e in ENGINES}
        self.sems = {}
        self.dma_pools = {}

    def sem(self, name):
        if name not in self.sems:
            self.sems[name] = self.es.enter_context(self.nc.semaphore(name))
            self.cnt[name] = 0
        return self.sems[name]

    def op(self, eng, fn, waits=(), inc=None, n=1):
        for tok in waits:
            if tok is None:
                continue
            s, v = tok
            w = self.waited[eng]
            if w.get(s, 0) < v:
                w[s] = v
                sem = self.sem(s)
                self.plan[eng].append(
                    lambda e, sem=sem, vv=v: e.wait_ge(sem, vv))
        token = None
        if inc is not None:
            sem = self.sem(inc)
            self.cnt[inc] += n
            token = (inc, self.cnt[inc])
            self.plan[eng].append(
                lambda e, fn=fn, sem=sem, nn=n: fn(e).then_inc(sem, nn))
        else:
            self.plan[eng].append(lambda e, fn=fn: fn(e))
        return token

    def wait(self, eng, tok):
        if tok is not None:
            s, v = tok
            w = self.waited[eng]
            if w.get(s, 0) < v:
                w[s] = v
                sem = self.sem(s)
                self.plan[eng].append(lambda e, sem=sem, vv=v: e.wait_ge(sem, vv))

    def dma(self, eng, fn, waits=(), stream="w", nsem=4):
        pool = self.dma_pools.setdefault(stream, {"n": nsem, "i": 0})
        idx = pool["i"] % pool["n"]
        pool["i"] += 1
        name = f"s_{stream}{idx}"
        prev_total = self.cnt.get(name, 0)
        chain = (name, prev_total) if prev_total else None
        return self.op(eng, fn, waits=list(waits) + [chain], inc=name, n=16)

    def flush_block(self):
        plans = self.plan
        self.plan = {e: [] for e in ENGINES}
        active = [(e, plans[e]) for e in ENGINES if plans[e]]
        if not active:
            return
        with self.nc.Block() as blk:
            for e, ops in active:
                def body(eng, ops=ops):
                    for f in ops:
                        f(eng)
                getattr(blk, e)(body)


def later(*toks):
    best = None
    for t in toks:
        if t is None:
            continue
        if best is None or t[1] > best[1]:
            best = t
    return best


def build(debug=False):
    nc = bass.Bass("TRN2", target_bir_lowering=False)

    xO = nc.dram_tensor("xO", [128, DCH * QT], F32, kind="ExternalInput").ap()
    xA = nc.dram_tensor("xA", [128, DCH * 512], F32, kind="ExternalInput").ap()
    xB = nc.dram_tensor("xB", [128, DCH * 512], F32, kind="ExternalInput").ap()
    wq_h = nc.dram_tensor("wq_h", [NH, 128, D], F32, kind="ExternalInput").ap()
    wk_h = nc.dram_tensor("wk_h", [NKV, 128, D], F32, kind="ExternalInput").ap()
    wv = nc.dram_tensor("wv", [D, 512], F32, kind="ExternalInput").ap()
    wo_h = nc.dram_tensor("wo_h", [DCH, 128, D], F32, kind="ExternalInput").ap()
    w0_h = nc.dram_tensor("w0_h", [NJ, 128, D], F32, kind="ExternalInput").ap()
    w1q = nc.dram_tensor("w1q", [32, 128, 4096], F32, kind="ExternalInput").ap()
    bq_c = nc.dram_tensor("bq_c", [128, NH], F32, kind="ExternalInput").ap()
    bk_c = nc.dram_tensor("bk_c", [128, NKV], F32, kind="ExternalInput").ap()
    bv_r = nc.dram_tensor("bv_r", [1, 512], F32, kind="ExternalInput").ap()
    bo_c = nc.dram_tensor("bo_c", [128, DCH], F32, kind="ExternalInput").ap()
    b0_c = nc.dram_tensor("b0_c", [128, NJ], F32, kind="ExternalInput").ap()
    b1_c = nc.dram_tensor("b1_c", [128, DCH], F32, kind="ExternalInput").ap()
    ones_cb = nc.dram_tensor("ones_cb", [128, 1], BF16, kind="ExternalInput").ap()
    ones_sq = nc.dram_tensor("ones_sq", [128, 128], BF16, kind="ExternalInput").ap()
    ones_rb = nc.dram_tensor("ones_rb", [1, 128], BF16, kind="ExternalInput").ap()
    ones_rf = nc.dram_tensor("ones_rf", [1, 128], F32, kind="ExternalInput").ap()
    consts = nc.dram_tensor("consts", [128, 4], F32, kind="ExternalInput").ap()
    outT = nc.dram_tensor("outT", [D, QT], F32, kind="ExternalOutput").ap()
    dbg = {}
    if debug:
        for name, total in [("d_kb", NKV * S), ("d_vb", 16 * 512),
                            ("d_qb", NH * QT), ("d_av", NH * QT),
                            ("d_res", DCH * QT), ("d_xh2", DCH * QT),
                            ("d_dact", 16 * 1024)]:
            dbg[name] = nc.dram_tensor(name, [128, total], F32,
                                       kind="ExternalOutput").ap()

    es = contextlib.ExitStack()
    with es:
        sc = Sched(nc, es)

        RES = es.enter_context(nc.sbuf_tensor("RES", [128, DCH * QT], F32))
        XH = es.enter_context(nc.sbuf_tensor("XH", [128, DCH * QT], BF16))
        bq_sb = es.enter_context(nc.sbuf_tensor("bq_sb", [128, NH], F32))
        bqs_sb = es.enter_context(nc.sbuf_tensor("bqs_sb", [128, NH], F32))
        bk_sb = es.enter_context(nc.sbuf_tensor("bk_sb", [128, NKV], F32))
        bv_sb = es.enter_context(nc.sbuf_tensor("bv_sb", [1, 512], BF16))
        bo_sb = es.enter_context(nc.sbuf_tensor("bo_sb", [128, DCH], F32))
        b0_sb = es.enter_context(nc.sbuf_tensor("b0_sb", [128, NJ], F32))
        b1_sb = es.enter_context(nc.sbuf_tensor("b1_sb", [128, DCH], F32))
        onescb = es.enter_context(nc.sbuf_tensor("onescb", [128, 1], BF16))
        onessq = es.enter_context(nc.sbuf_tensor("onessq", [128, 128], BF16))
        onesrb = es.enter_context(nc.sbuf_tensor("onesrb", [1, 128], BF16))
        onesrf = es.enter_context(nc.sbuf_tensor("onesrf", [1, 128], F32))
        cst_sb = es.enter_context(nc.sbuf_tensor("cst_sb", [128, 4], F32))
        nc.const_aps.register(F32, EPS, cst_sb[:, 0:1])
        nc.const_aps.register(F32, NEGHALF, cst_sb[:, 1:2])
        nc.const_aps.register(F32, SCALE, cst_sb[:, 2:3])
        nc.const_aps.register(F32, NEG1, cst_sb[:, 3:4])

        t_small = []
        for dst, src in [(bq_sb[:, :], bq_c), (bk_sb[:, :], bk_c),
                         (bo_sb[:, :], bo_c), (b0_sb[:, :], b0_c),
                         (b1_sb[:, :], b1_c), (onescb[:, :], ones_cb),
                         (onessq[:, :], ones_sq),
                         (onesrb[:, :], ones_rb), (onesrf[:, :], ones_rf),
                         (cst_sb[:, :], consts)]:
            t_small.append(sc.dma("sync", lambda e, d=dst, s=src: e.dma_start(d, s),
                                  stream="sm", nsem=4))
        tok_small_all = [t_small[-4], t_small[-3], t_small[-2], t_small[-1]]
        t_bv = sc.dma("gpsimd", lambda e: e.dma_start(bv_sb[:, :], bv_r),
                      stream="w")
        t_bqs = sc.op("vector",
                      lambda e: e.tensor_scalar_mul(bqs_sb[:, :], bq_sb[:, :], SCALE),
                      waits=tok_small_all, inc="s_v")

        xb = es.enter_context(nc.sbuf_tensor("xb", [128, 1024], BF16))
        x2b = es.enter_context(nc.sbuf_tensor("x2b", [128, 1024], BF16))
        abc = es.enter_context(nc.sbuf_tensor("abc", [128, 1024], BF16))
        bbc = es.enter_context(nc.sbuf_tensor("bbc", [128, 1024], BF16))
        tmp = (es.enter_context(nc.sbuf_tensor("tmp", [128, 512], F32))
               if debug else None)
        tmp2 = es.enter_context(nc.sbuf_tensor("tmp2", [128, 1024], BF16))
        meanF = es.enter_context(nc.sbuf_tensor("meanF", [128, 512], F32))
        scrF = es.enter_context(nc.sbuf_tensor("scrF", [128, 512], F32))

        s1w = es.enter_context(contextlib.ExitStack())
        wblk = s1w.enter_context(nc.sbuf_tensor("wblk", [128, 4 * D], BF16))
        s1kv = es.enter_context(contextlib.ExitStack())
        KB = s1kv.enter_context(nc.sbuf_tensor("KB", [128, NKV * S], BF16))
        VB = s1kv.enter_context(nc.sbuf_tensor("VB", [128, 16 * 512], BF16))

        wblk_free = [None, None, None, None]

        def load_wblk_at(slot, src2d):
            dst = wblk[:, slot * D:(slot + 1) * D]
            return sc.dma("gpsimd", lambda e, d=dst, s=src2d: e.dma_start(d, s),
                          waits=[wblk_free[slot]], stream="w")

        def emit_ln(src_of, dst_off, width, tag, src_ready_of=None,
                    mm_extra=None):
            nsub = max(1, width // 512)
            subw = min(512, width)
            ps = contextlib.ExitStack()
            t_rstd = {}
            t_cpa = {}
            t_cpb = {}
            with ps:
                SA = ps.enter_context(nc.psum_tensor("psA" + tag, [128, width], F32))
                SB = ps.enter_context(nc.psum_tensor("psB" + tag, [128, width], F32))
                mm_toks = []
                sub_done = {}
                for dc in range(DCH):
                    rdy = list(src_ready_of(dc)) if src_ready_of else []
                    for sub in range(nsub):
                        i = dc * nsub + sub
                        seg = i % 2
                        xb_seg = xb[:, seg * 512:seg * 512 + subw]
                        x2b_seg = x2b[:, seg * 512:seg * 512 + subw]
                        src_seg = src_of(dc)[:, sub * subw:(sub + 1) * subw]
                        guard = mm_toks[-2] if len(mm_toks) >= 2 else None
                        t_cp = sc.op("vector",
                                     lambda e, d=xb_seg, s=src_seg: e.tensor_copy(d, s),
                                     waits=rdy + [guard], inc="s_v")
                        t_sq = sc.op("scalar",
                                     lambda e, d=x2b_seg, s=src_seg:
                                     e.activation(d, s, AF.Square),
                                     waits=rdy + [guard], inc="s_a")
                        st = (dc == 0)
                        sp = (dc == DCH - 1)
                        sc.op("tensor",
                              lambda e, d=SA[:, sub * subw:(sub + 1) * subw],
                              r=xb_seg, st=st, sp=sp:
                              e.matmul(d, onessq[:, :], r, start=st, stop=sp),
                              waits=[t_cp, mm_extra] + tok_small_all)
                        mm_toks.append(sc.op(
                            "tensor",
                            lambda e, d=SB[:, sub * subw:(sub + 1) * subw],
                            r=x2b_seg, st=st, sp=sp:
                            e.matmul(d, onessq[:, :], r, start=st, stop=sp),
                            waits=[t_sq], inc="s_t"))
                        if sp:
                            sub_done[sub] = mm_toks[-1]
                for sub in range(nsub):
                    lo = sub * subw
                    sl = slice(lo, lo + subw)
                    s0 = slice(0, subw)
                    ssl = slice(sub * subw, (sub + 1) * subw)
                    t_mean = sc.op("vector",
                                   lambda e, s0=s0, ssl=ssl: e.tensor_scalar_mul(
                                       meanF[:, s0], SA[:, ssl], INV_D),
                                   waits=[sub_done[sub]], inc="s_v")
                    t_ex2 = sc.op("vector",
                                  lambda e, s0=s0, ssl=ssl: e.tensor_scalar_mul(
                                      scrF[:, s0], SB[:, ssl], INV_D),
                                  waits=[t_mean], inc="s_v")
                    t_nm2 = sc.op("vector",
                                  lambda e, s0=s0, ssl=ssl: e.scalar_tensor_tensor(
                                      SA[:, ssl], meanF[:, s0], -1.0,
                                      meanF[:, s0], ALU.mult, ALU.mult),
                                  waits=[t_ex2], inc="s_v")
                    t_var = sc.op("vector",
                                  lambda e, s0=s0, ssl=ssl: e.tensor_tensor(
                                      SB[:, ssl], scrF[:, s0],
                                      SA[:, ssl], ALU.add),
                                  waits=[t_nm2], inc="s_v")
                    t_lnv = sc.op("scalar",
                                  lambda e, ssl=ssl: e.activation(
                                      SA[:, ssl], SB[:, ssl], AF.Ln, bias=EPS),
                                  waits=[t_var], inc="s_a")
                    t_rstd[sub] = sc.op("scalar",
                                        lambda e, s0=s0, ssl=ssl: e.activation(
                                            scrF[:, s0], SA[:, ssl], AF.Exp,
                                            scale=NEGHALF),
                                        waits=[t_lnv, t_cpb.get(sub - 1)],
                                        inc="s_a")
                    t_cpa[sub] = sc.op("vector",
                                       lambda e, sl=sl, s0=s0: e.tensor_copy(
                                           abc[:, sl], scrF[:, s0]),
                                       waits=[t_rstd[sub]], inc="s_v")
                    t_cpb[sub] = sc.op("vector",
                                       lambda e, sl=sl, s0=s0:
                                       e.scalar_tensor_tensor(
                                           bbc[:, sl], meanF[:, s0], -1.0,
                                           scrF[:, s0], ALU.mult, ALU.mult),
                                       waits=[t_cpa[sub]], inc="s_v")
            adds = []
            adds_map = {}
            for sub in range(nsub):
                lo = sub * subw
                for dc in range(DCH):
                    seg = dc % 2
                    tseg = tmp2[:, seg * 512:seg * 512 + subw]
                    src_seg = src_of(dc)[:, lo:lo + subw]
                    dst = XH[:, dst_off(dc) + lo: dst_off(dc) + lo + subw]
                    guard = adds[-2] if len(adds) >= 2 else None
                    last_mul = sc.op("vector",
                                     lambda e, t=tseg, s=src_seg, lo=lo:
                                     e.tensor_tensor(t, s, abc[:, lo:lo + subw],
                                                     ALU.mult),
                                     waits=[t_cpa[sub], t_cpb[sub], guard],
                                     inc="s_v")
                    last_add = sc.op("vector",
                                     lambda e, d=dst, t=tseg, lo=lo:
                                     e.tensor_tensor(d, t, bbc[:, lo:lo + subw],
                                                     ALU.add),
                                     waits=[last_mul], inc="s_v")
                    adds.append(last_add)
                    adds_map[(sub, dc)] = last_add
            return adds[-1], adds_map, t_rstd[0], t_rstd[nsub - 1]

        def emit_k(half, sub, loads, ps_k, k_ev, adds, bank_toks):
            for kc in range(NKV):
                slot, t_load = loads[kc]
                slot2 = kc % 2
                t_mm = None
                for dc in range(DCH):
                    t_mm = sc.op(
                        "tensor",
                        lambda e, slot=slot, slot2=slot2, dc=dc, sub=sub:
                        e.matmul(
                            ps_k[slot2][:, sub * 512:(sub + 1) * 512],
                            wblk[:, slot * D + dc * 128:slot * D + (dc + 1) * 128],
                            XH[:, dc * 1024 + sub * 512:dc * 1024 + (sub + 1) * 512],
                            start=(dc == 0), stop=(dc == DCH - 1)),
                        waits=[t_load, adds[(sub, dc)],
                               k_ev.get((slot2, sub))]
                              + (bank_toks if (kc == 0 and dc == 0) else []),
                        inc="s_t" if dc == DCH - 1 else None)
                k_ev[(slot2, sub)] = sc.op(
                    "scalar",
                    lambda e, kc=kc, slot2=slot2, sub=sub, half=half:
                    e.activation(
                        KB[:, kc * S + half * 1024 + sub * 512:
                           kc * S + half * 1024 + (sub + 1) * 512],
                        ps_k[slot2][:, sub * 512:(sub + 1) * 512],
                        AF.Identity, bias=bk_sb[:, kc:kc + 1]),
                    waits=[t_mm], inc="s_a")
                if sub == 1:
                    wblk_free[slot] = t_mm

        def emit_v(half, wvst, adds, bank_toks, k_done):
            psV = contextlib.ExitStack()
            with psV:
                ps_v = [psV.enter_context(
                    nc.psum_tensor(f"ps_v{half}{k}", [128, 512], F32))
                    for k in range(8)]
                v_ev = [None] * 8
                wv_free = [None] * 4
                wv_loaded = {}

                def load_wv(dc):
                    slot = dc % 4
                    wv_loaded[dc] = sc.dma(
                        "gpsimd",
                        lambda e, dc=dc, slot=slot: e.dma_start(
                            wvst[:, slot * 512:(slot + 1) * 512],
                            wv[dc * 128:(dc + 1) * 128, :]),
                        waits=[wv_free[slot]], stream="w")

                for dc in range(4):
                    load_wv(dc)
                t_last = None
                for dc in range(DCH):
                    slot = dc % 4
                    for j in range(8):
                        t_last = sc.op(
                            "tensor",
                            lambda e, dc=dc, kt=j, slot=slot, j=j:
                            e.matmul(
                                ps_v[j][:, :],
                                XH[:, dc * 1024 + kt * 128:dc * 1024 + kt * 128 + 128],
                                wvst[:, slot * 512:(slot + 1) * 512],
                                start=(dc == 0), stop=False),
                            waits=[wv_loaded[dc], adds[(j // 4, dc)],
                                   k_done if dc == 0 else None]
                                  + (bank_toks if dc == 0 else []),
                            inc="s_t" if j == 7 else None)
                    wv_free[slot] = t_last
                    if dc + 4 < DCH:
                        load_wv(dc + 4)
                for j in range(8):
                    gkt = half * 8 + j
                    t_r1 = sc.op("tensor",
                                 lambda e, j=j: e.matmul(
                                     ps_v[j][:, :], onesrb[:, :], bv_sb[:, :],
                                     start=False, stop=True),
                                 waits=[t_last, t_bv], inc="s_t")
                    v_ev[j] = sc.op(
                        "scalar",
                        lambda e, gkt=gkt, j=j: e.activation(
                            VB[:, gkt * 512:(gkt + 1) * 512],
                            ps_v[j][:, :], AF.Copy),
                        waits=[t_r1], inc="s_a")
            return v_ev

        sA = es.enter_context(contextlib.ExitStack())
        XS = sA.enter_context(nc.sbuf_tensor("XS", [128, DCH * 512], F32))
        wvstA = sA.enter_context(nc.sbuf_tensor("wvstA", [128, 4 * 512], BF16))

        psKA = es.enter_context(contextlib.ExitStack())
        ps_kA = [psKA.enter_context(nc.psum_tensor(f"ps_k1{p}", [128, 1024], F32))
                 for p in range(2)]
        k_evA = {}
        loadsA = {kc: (kc, load_wblk_at(kc, wk_h[kc])) for kc in range(NKV)}
        adds_other = {}
        bank_toks_A = []
        t_res_load = None
        prev_amap = None
        for stripe in range(2):
            xsrc = xA if stripe == 0 else xB
            t_xs = {}
            for dc in range(DCH):
                w = [prev_amap[(0, dc)]] if prev_amap else []
                t_xs[dc] = sc.dma(
                    "sync",
                    lambda e, dc=dc, xsrc=xsrc: e.dma_start(
                        XS[:, dc * 512:(dc + 1) * 512],
                        xsrc[:, dc * 512:(dc + 1) * 512]),
                    waits=w, stream="x", nsem=4)
            if stripe == 0:
                t_res_load = sc.dma(
                    "sync",
                    lambda e: e.dma_start(RES[:, :], xO),
                    stream="x", nsem=4)
            _, amap, t_ca, t_cb = emit_ln(
                lambda dc: XS[:, dc * 512:(dc + 1) * 512],
                lambda dc, s=stripe: dc * 1024 + s * 512,
                512, f"A{stripe}", src_ready_of=lambda dc: [t_xs[dc]],
                mm_extra=(bank_toks_A[1] if stripe else None))
            for dc in range(DCH):
                adds_other[(stripe, dc)] = amap[(0, dc)]
            bank_toks_A = [t_ca, t_cb]
            prev_amap = amap
            emit_k(1, stripe, loadsA, ps_kA, k_evA, adds_other, [])
        psKA.close()
        k_other = list(k_evA.values())
        kv_other = emit_v(1, wvstA, adds_other, bank_toks_A,
                          later(*k_other)) + k_other

        sA.close()
        sB = es.enter_context(contextlib.ExitStack())
        wvstB = sB.enter_context(nc.sbuf_tensor("wvstB", [128, 4 * 512], BF16))
        QB = sB.enter_context(nc.sbuf_tensor("QB", [128, NH * QT], BF16))

        kv_other_done = later(*kv_other)
        xh_own, adds_own, t_caB, t_cbB = emit_ln(
            lambda dc: RES[:, dc * 1024:(dc + 1) * 1024],
            lambda dc: dc * 1024, 1024, "B",
            src_ready_of=lambda dc: [t_res_load] + bank_toks_A,
            mm_extra=kv_other_done)
        psKB = es.enter_context(contextlib.ExitStack())
        ps_kB = [psKB.enter_context(nc.psum_tensor(f"ps_k0{p}", [128, 1024], F32))
                 for p in range(2)]
        k_evB = {}
        loadsB = {kc: (kc, load_wblk_at(kc, wk_h[kc])) for kc in range(NKV)}
        emit_k(0, 0, loadsB, ps_kB, k_evB, adds_own, [t_caB, t_cbB])
        emit_k(0, 1, loadsB, ps_kB, k_evB, adds_own, [t_caB, t_cbB])
        psKB.close()
        k_own = list(k_evB.values())
        kv_own = emit_v(0, wvstB, adds_own, [t_caB, t_cbB],
                        later(*k_own)) + k_own

        psQ = contextlib.ExitStack()
        with psQ:
            ps_q = [psQ.enter_context(nc.psum_tensor(f"ps_q{p}", [128, 1024], F32))
                    for p in range(2)]
            q_ev = {}
            kv_own_done = later(*kv_own)
            qi = [0]
            qloads = {}
            qslots = {}

            def qload(hc):
                s3 = qi[0] % 4
                qloads[hc] = load_wblk_at(s3, wq_h[hc])
                qslots[hc] = s3
                qi[0] += 1

            qload(0)
            qload(1)
            qload(2)
            for hc in range(NH):
                slot4 = qslots[hc]
                par = hc % 2
                t_mm = None
                for sub in range(2):
                    for dc in range(DCH):
                        t_mm = sc.op(
                            "tensor",
                            lambda e, slot4=slot4, par=par, dc=dc, sub=sub:
                            e.matmul(
                                ps_q[par][:, sub * 512:(sub + 1) * 512],
                                wblk[:, slot4 * D + dc * 128:slot4 * D + (dc + 1) * 128],
                                XH[:, dc * 1024 + sub * 512:dc * 1024 + (sub + 1) * 512],
                                start=(dc == 0), stop=(dc == DCH - 1)),
                            waits=[qloads[hc], adds_own[(sub, dc)],
                                   q_ev.get((par, sub)), kv_own_done],
                            inc="s_t" if dc == DCH - 1 else None)
                    q_ev[(par, sub)] = sc.op(
                        "scalar",
                        lambda e, hc=hc, par=par, sub=sub: e.activation(
                            QB[:, hc * QT + sub * 512:hc * QT + sub * 512 + 512],
                            ps_q[par][:, sub * 512:(sub + 1) * 512],
                            AF.Identity, bias=bqs_sb[:, hc:hc + 1], scale=SCALE),
                        waits=[t_mm, t_bqs], inc="s_a")
                wblk_free[slot4] = t_mm
                if hc + 3 < NH:
                    qload(hc + 3)
            q_ev = list(q_ev.values())

        oloads = {d: load_wblk_at(d % 4, wo_h[d]) for d in range(4)}

        sC = es.enter_context(contextlib.ExitStack())
        NPT = 4 if debug else 6
        PT = sC.enter_context(nc.sbuf_tensor("PT", [128, NPT * 512], BF16))
        rc_sb = sC.enter_context(nc.sbuf_tensor("rc_sb", [128, 1024], F32))
        ps_st = sC.enter_context(nc.psum_tensor("ps_st", [128, 4 * 512], F32))
        ps_u = sC.enter_context(nc.psum_tensor("ps_u", [128, 2 * 512], F32))
        ps_bc = sC.enter_context(nc.psum_tensor("ps_bc", [128, 2 * 512], F32))

        NIT = 32

        T = {k: [None] * NIT for k in ("rc", "avmul")}
        Texp = [[None] * 16 for _ in range(NIT)]
        Tst = [[None] * 16 for _ in range(NIT)]
        Tus = [[None] * 16 for _ in range(NIT)]

        def q_off(h):
            return (h >> 1) * QT + (h & 1) * 512

        def kv_of(h):
            return (h >> 1) // 4

        def emit_st(h, c):
            qo, kv = q_off(h), kv_of(h)
            bank_guard = (Texp[h][c - 4] if c >= 4 else
                          (Texp[h - 1][12 + c] if h >= 1 else qev_last))
            Tst[h][c] = sc.op(
                "tensor",
                lambda e, c=c, qo=qo, kv=kv:
                e.matmul(ps_st[:, (c % 4) * 512:(c % 4) * 512 + 512],
                         KB[:, kv * S + c * 128:kv * S + (c + 1) * 128],
                         QB[:, qo:qo + 512], start=True, stop=True),
                waits=[bank_guard], inc="s_t")

        def emit_exp(h, c):
            ring_guard = (Tus[h][c - NPT] if c >= NPT else
                          (Tus[h - 1][16 - NPT + c] if h >= 1 else None))
            Texp[h][c] = sc.op(
                "scalar",
                lambda e, c=c: e.activation(
                    PT[:, (c % NPT) * 512:(c % NPT) * 512 + 512],
                    ps_st[:, (c % 4) * 512:(c % 4) * 512 + 512],
                    AF.Exp),
                waits=[Tst[h][c], ring_guard],
                inc="s_a")

        def emit_usums(h, c):
            kv = kv_of(h)
            par = h % 2
            pt_sl = PT[:, (c % NPT) * 512:(c % NPT) * 512 + 512]
            sc.op(
                "tensor",
                lambda e, c=c, kv=kv, par=par, pt=pt_sl:
                e.matmul(ps_u[:, par * 512:par * 512 + 512],
                         VB[:, c * 512 + kv * 128:c * 512 + kv * 128 + 128],
                         pt, start=(c == 0), stop=(c == 15)),
                waits=[Texp[h][c],
                       (T["avmul"][h - 2] if h >= 2 else qev_last)
                       if c == 0 else None])
            Tus[h][c] = sc.op(
                "tensor",
                lambda e, c=c, par=par, pt=pt_sl:
                e.matmul(ps_bc[:, par * 512:par * 512 + 512],
                         onessq[:, :], pt,
                         start=(c == 0), stop=(c == 15)),
                waits=[(T["rc"][h - 2] if h >= 2 else qev_last)
                       if c == 0 else None],
                inc="s_t")

        def emit_tail(h):
            head, qt = h >> 1, h & 1
            par = h % 2
            av = XH[:, head * QT + qt * 512: head * QT + qt * 512 + 512]
            T["rc"][h] = sc.op(
                "vector",
                lambda e, par=par: e.reciprocal(
                    rc_sb[:, par * 512:par * 512 + 512],
                    ps_bc[:, par * 512:par * 512 + 512]),
                waits=[Tus[h][15]], inc="s_v")
            T["avmul"][h] = sc.op(
                "vector",
                lambda e, av=av, par=par: e.tensor_tensor(
                    av, ps_u[:, par * 512:par * 512 + 512],
                    rc_sb[:, par * 512:par * 512 + 512], ALU.mult),
                waits=[T["rc"][h]], inc="s_v")

        def dump(name, src_sb, total, ready_toks):
            st = tmp
            prev = None
            for off in range(0, total, 512):
                t_c = sc.op("vector",
                            lambda e, o=off, st=st:
                            e.tensor_copy(st[:, :], src_sb[:, o:o + 512]),
                            waits=list(ready_toks) + [prev], inc="s_v")
                prev = sc.dma("sync",
                              lambda e, o=off, st=st:
                              e.dma_start(dbg[name][:, o:o + 512], st[:, :]),
                              waits=[t_c], stream="dbg", nsem=2)
            sc.wait("sync", prev)
            sc.flush_block()

        if debug:
            sc.flush_block()
            dump("d_kb", KB, NKV * S, kv_other + kv_own)
            dump("d_vb", VB, 16 * 512, kv_other + kv_own)
            dump("d_qb", QB, NH * QT, q_ev)

        qev_last = later(*(q_ev + kv_own))
        for h in range(NIT):
            for c in range(16):
                emit_st(h, c)
                if c >= 2:
                    emit_usums(h, c - 2)
                emit_exp(h, c)
            emit_usums(h, 14)
            emit_usums(h, 15)
            emit_tail(h)

        sc.flush_block()
        if debug:
            dump("d_av", XH, NH * QT, [T["avmul"][NIT - 1]])
        sC.close()

        sB.close()
        s1kv.close()
        sF = es.enter_context(contextlib.ExitStack())
        Dact = sF.enter_context(nc.sbuf_tensor("Dact", [128, 16 * 1024], BF16))
        w0st = sF.enter_context(nc.sbuf_tensor("w0st", [128, 3 * D], BF16))
        w1st = sF.enter_context(nc.sbuf_tensor("w1st", [128, 2 * 4096], BF16))

        arena_tok = later(Tst[NIT - 1][15], Tus[NIT - 1][15])
        arena_tok_v = T["avmul"][NIT - 1]

        w0_free = [None] * 3
        w1_free = [None] * 2
        w0_loaded = {}
        w1_loaded = {}

        def load_w0(nj):
            slot = nj % 3
            w0_loaded[nj] = sc.dma(
                "gpsimd",
                lambda e, nj=nj, slot=slot: e.dma_start(
                    w0st[:, slot * D:(slot + 1) * D], w0_h[nj]),
                waits=[w0_free[slot], arena_tok, arena_tok_v], stream="w")

        def load_w1(q, og):
            slot = og % 2
            w1_loaded[(q, og)] = sc.dma(
                "gpsimd",
                lambda e, q=q, og=og, slot=slot: e.dma_start(
                    w1st[:, slot * 4096:(slot + 1) * 4096], w1q[q * 8 + og]),
                waits=[w1_free[slot], arena_tok, arena_tok_v], stream="w1",
                nsem=4)

        av_all = T["avmul"]
        res1_ev = [None] * DCH
        o_last_mm = None
        psO = contextlib.ExitStack()
        with psO:
            ps_o = [psO.enter_context(nc.psum_tensor(f"ps_o{p}", [128, 1024], F32))
                    for p in range(2)]
            o_ev = [None, None]
            for dc in range(DCH):
                slot4 = dc % 4
                slot = dc % 2
                t_mm = None
                for hc in range(NH):
                    for sub in range(2):
                        t_mm = sc.op(
                            "tensor",
                            lambda e, slot4=slot4, slot=slot, hc=hc, sub=sub:
                            e.matmul(
                                ps_o[slot][:, sub * 512:(sub + 1) * 512],
                                wblk[:, slot4 * D + hc * 128:slot4 * D + (hc + 1) * 128],
                                XH[:, hc * QT + sub * 512:hc * QT + (sub + 1) * 512],
                                start=(hc == 0), stop=(hc == NH - 1)),
                            waits=[oloads[dc], av_all[2 * hc + sub], o_ev[slot],
                                   T["avmul"][NIT - 1] if (dc == 0 and hc == 0)
                                   else None],
                            inc="s_t" if (hc == NH - 1 and sub == 1) else None)
                wblk_free[slot4] = t_mm
                o_last_mm = t_mm
                if dc + 4 < DCH:
                    oloads[dc + 4] = load_wblk_at((dc + 4) % 4, wo_h[dc + 4])
                if dc == 0:
                    load_w0(0)
                elif dc == 1:
                    load_w1(0, 0)
                elif dc == 3:
                    load_w0(1)
                elif dc == 5:
                    load_w1(0, 1)
                elif dc == 7:
                    load_w0(2)
                o_ev[slot] = sc.op(
                    "vector",
                    lambda e, dc=dc, slot=slot: e.scalar_tensor_tensor(
                        RES[:, dc * QT:(dc + 1) * QT], ps_o[slot][:, :],
                        bo_sb[:, dc:dc + 1], RES[:, dc * QT:(dc + 1) * QT],
                        ALU.add, ALU.add),
                    waits=[t_mm], inc="s_v")
                res1_ev[dc] = o_ev[slot]

            if debug:
                sc.flush_block()
                dump("d_res", RES, DCH * QT, [later(*res1_ev)])

            xh2, adds2, t_caE, t_cbE = emit_ln(
                lambda dc: RES[:, dc * 1024:(dc + 1) * 1024],
                lambda dc: dc * 1024, 1024, "C",
                src_ready_of=lambda dc: [res1_ev[dc]])

        if debug:
            sc.flush_block()
            dump("d_xh2", XH, DCH * QT, [xh2])

        ps_f1 = [sF.enter_context(nc.psum_tensor(f"ps_f1{p}", [128, 512], F32))
                 for p in range(4)]
        ps_f2 = [sF.enter_context(nc.psum_tensor(f"ps_f2{p}", [128, 512], F32))
                 for p in range(4)]

        f1_ev = {}
        f1_mm = {}
        f2_bank_free = {}
        f2_last_prev = None
        out_last = None

        for q in range(4):
            for j16 in range(16):
                nj = q * 16 + j16
                slot = nj % 3
                par = j16 % 2
                t_mm_sub = [None, None]
                for dc in range(DCH):
                    for sub in range(2):
                        t_mm_sub[sub] = sc.op(
                            "tensor",
                            lambda e, slot=slot, dc=dc, par=par, sub=sub:
                            e.matmul(
                                ps_f1[par * 2 + sub][:, :],
                                w0st[:, slot * D + dc * 128:slot * D + (dc + 1) * 128],
                                XH[:, dc * 1024 + sub * 512:dc * 1024 + sub * 512 + 512],
                                start=(dc == 0), stop=(dc == DCH - 1)),
                            waits=[w0_loaded[nj], adds2[(sub, dc)],
                                   f1_ev.get((nj - 2, sub)),
                                   t_caE if (q == 0 and j16 < 2 and dc == 0)
                                   else None,
                                   t_cbE if (q == 0 and j16 < 2 and dc == 0)
                                   else None],
                            inc="s_t" if dc == DCH - 1 else None)
                w0_free[slot] = t_mm_sub[1]
                if nj + 3 < NJ:
                    load_w0(nj + 3)
                for sub in range(2):
                    f1_ev[(nj, sub)] = sc.op(
                        "scalar",
                        lambda e, nj=nj, j16=j16, par=par, sub=sub: e.activation(
                            Dact[:, j16 * 1024 + sub * 512:j16 * 1024 + sub * 512 + 512],
                            ps_f1[par * 2 + sub][:, :],
                            AF.Gelu, bias=b0_sb[:, nj:nj + 1]),
                        waits=[t_mm_sub[sub], f2_last_prev], inc="s_a")

            if debug and q == 0:
                sc.flush_block()
                dump("d_dact", Dact, 16 * 1024, [f1_ev[(15, 1)]])

            for og in range(8):
                slot = og % 2
                t_grp = None
                for o2 in range(2):
                    for sub in range(2):
                        bank = o2 * 2 + sub
                        for j16 in range(16):
                            nj = q * 16 + j16
                            t_grp = sc.op(
                                "tensor",
                                lambda e, slot=slot, j16=j16, o2=o2, sub=sub,
                                bank=bank:
                                e.matmul(
                                    ps_f2[bank][:, :],
                                    w1st[:, slot * 4096 + j16 * 256 + o2 * 128:
                                         slot * 4096 + j16 * 256 + o2 * 128 + 128],
                                    Dact[:, j16 * 1024 + sub * 512:
                                         j16 * 1024 + sub * 512 + 512],
                                    start=(j16 == 0), stop=(j16 == 15)),
                                waits=[w1_loaded[(q, og)], f1_ev[(nj, sub)],
                                       f2_bank_free.get(bank) if j16 == 0
                                       else None],
                                inc="s_t" if j16 == 15 else None)
                        hc = og * 2 + o2
                        rs = RES[:, hc * QT + sub * 512:hc * QT + sub * 512 + 512]
                        if q == 0:
                            t_ev = sc.op(
                                "vector",
                                lambda e, rs=rs, bank=bank, hc=hc:
                                e.scalar_tensor_tensor(
                                    rs, ps_f2[bank][:, :], b1_sb[:, hc:hc + 1],
                                    rs, ALU.add, ALU.add),
                                waits=[t_grp], inc="s_v")
                        else:
                            t_ev = sc.op(
                                "vector",
                                lambda e, rs=rs, bank=bank:
                                e.tensor_tensor(rs, ps_f2[bank][:, :], rs,
                                                ALU.add),
                                waits=[t_grp], inc="s_v")
                        f2_bank_free[bank] = t_ev
                        if q == 3 and sub == 1:
                            out_last = sc.dma(
                                "sync",
                                lambda e, hc=hc: e.dma_start(
                                    outT[hc * 128:(hc + 1) * 128, :],
                                    RES[:, hc * QT:(hc + 1) * QT]),
                                waits=[t_ev], stream="o", nsem=2)
                w1_free[slot] = t_grp
                if og + 2 < 8:
                    load_w1(q, og + 2)
                elif q < 3:
                    load_w1(q + 1, og - 6)
            f2_last_prev = t_grp

        for idx in range(sc.dma_pools.get("o", {"n": 2})["n"]):
            name = f"s_o{idx}"
            if name in sc.cnt:
                sc.wait("sync", (name, sc.cnt[name]))
        sc.flush_block()

    return nc


def _host_prep(inp):
    f32 = np.float32
    x = np.asarray(inp["L0_attn_x"], f32)
    Wq = np.asarray(inp["L0_attn_Wq"], f32)
    bq = np.asarray(inp["L0_attn_bq"], f32)
    Wk = np.asarray(inp["L0_attn_Wk"], f32)
    bk = np.asarray(inp["L0_attn_bk"], f32)
    Wv = np.asarray(inp["L0_attn_Wv"], f32)
    bv = np.asarray(inp["L0_attn_bv"], f32)
    Wo = np.asarray(inp["L0_attn_Wo"], f32)
    bo = np.asarray(inp["L0_attn_bo"], f32)
    W0 = np.asarray(inp["L0_ffn_W0"], f32)
    b0 = np.asarray(inp["L0_ffn_bias0"], f32)
    W1 = np.asarray(inp["L0_ffn_W1"], f32)
    b1 = np.asarray(inp["L0_ffn_bias1"], f32)

    shared = {
        "wq_h": np.ascontiguousarray(
            Wq.reshape(DCH, 128, NH, 128).transpose(2, 1, 0, 3).reshape(NH, 128, D)),
        "wk_h": np.ascontiguousarray(
            Wk.reshape(DCH, 128, NKV, 128).transpose(2, 1, 0, 3).reshape(NKV, 128, D)),
        "wv": np.ascontiguousarray(Wv),
        "wo_h": np.ascontiguousarray(
            Wo.reshape(NH, 128, DCH, 128).transpose(2, 1, 0, 3).reshape(DCH, 128, D)),
        "w0_h": np.ascontiguousarray(
            W0.reshape(NJ, 128, DCH, 128).transpose(0, 3, 2, 1).reshape(NJ, 128, D)),
        "w1q": np.ascontiguousarray(
            W1.reshape(8, 2, 128, 4, 16, 128).transpose(3, 0, 5, 4, 1, 2)
            .reshape(32, 128, 4096)),
        "bq_c": np.ascontiguousarray(bq.reshape(NH, 128).T),
        "bk_c": np.ascontiguousarray(bk.reshape(NKV, 128).T),
        "bv_r": np.ascontiguousarray(bv.reshape(1, 512)),
        "bo_c": np.ascontiguousarray(bo.reshape(DCH, 128).T),
        "b0_c": np.ascontiguousarray(b0.reshape(NJ, 128).T),
        "b1_c": np.ascontiguousarray(b1.reshape(DCH, 128).T),
        "ones_cb": np.ones((128, 1), ml_dtypes.bfloat16),
        "ones_sq": np.ones((128, 128), ml_dtypes.bfloat16),
        "ones_rb": np.ones((1, 128), ml_dtypes.bfloat16),
        "ones_rf": np.ones((1, 128), f32),
        "consts": np.broadcast_to(
            np.array([EPS, NEGHALF, SCALE, NEG1], f32), (128, 4)).copy(),
    }
    in_maps = []
    for c in range(8):
        b, h = c // 2, c % 2
        A = np.ascontiguousarray(x[b].reshape(S, DCH, 128).transpose(2, 1, 0))
        o0 = (1 - h) * QT
        m = dict(shared)
        m["xO"] = np.ascontiguousarray(
            A[:, :, h * QT:(h + 1) * QT]).reshape(128, DCH * QT)
        m["xA"] = np.ascontiguousarray(
            A[:, :, o0:o0 + 512]).reshape(128, DCH * 512)
        m["xB"] = np.ascontiguousarray(
            A[:, :, o0 + 512:o0 + 1024]).reshape(128, DCH * 512)
        in_maps.append(m)
    return in_maps


def kernel(**inp):
    in_maps = _host_prep(inp)
    nc = build()
    res = run_bass_kernel_spmd(nc, in_maps, core_ids=list(range(8)))
    out = np.empty((4, S, D), np.float32)
    for c in range(8):
        b, h = c // 2, c % 2
        out[b, h * QT:(h + 1) * QT, :] = res.results[c]["outT"].T
    return out
